# version 07
# baseline (speedup 1.0000x reference)
import numpy as np
import ml_dtypes
import concourse.bass as bass
import concourse.mybir as mybir
from concourse.bass_utils import run_bass_kernel_spmd

F32 = mybir.dt.float32
F16 = mybir.dt.float16
BF16 = mybir.dt.bfloat16
I32 = mybir.dt.int32
ALU = mybir.AluOpType
ACT = mybir.ActivationFunctionType
AX = mybir.AxisListType

B, S, D, H, DH, F = 4, 4096, 2048, 16, 128, 8192
TOK = 2048
SCALE = 1.0 / float(np.sqrt(DH))
SHIFT = -30.0
EPS = 1e-5
NCHUNK = S // 128
NOWN = TOK // 128
QG = 256
NQG = TOK // QG
NKT = S // 128
C_ONE, C_SCALE, C_SHIFT, C_IMEAN, C_127, C_HALF, C_NHALF, C_EPS = range(8)


class Sem:
    def __init__(self, handle):
        self.h = handle
        self.n = 0


class Builder:

    def __init__(self):
        self.streams = {k: [] for k in ("sync", "scalar", "vector", "tensor")}

    def op(self, eng, fn, inc=None, k=1):
        if inc is None:
            self.streams[eng].append(fn)
            return None
        def wrapped(e, fn=fn, sem=inc, k=k):
            fn(e).then_inc(sem.h, k)
        self.streams[eng].append(wrapped)
        inc.n += k
        return inc.n

    def wait(self, eng, sem, val):
        if val and val > 0:
            self.streams[eng].append(lambda e, s=sem, v=val: e.wait_ge(s.h, v))


def build_core():
    nc = bass.Bass("TRN2", target_bir_lowering=False, debug=False)

    xb = nc.dram_tensor("xb", [S, D], F32, kind="ExternalInput").ap()
    qw = nc.dram_tensor("qw", [D, D], F32, kind="ExternalInput").ap()
    kw = nc.dram_tensor("kw", [DH, D], F32, kind="ExternalInput").ap()
    vw = nc.dram_tensor("vw", [DH, D], F32, kind="ExternalInput").ap()
    ow = nc.dram_tensor("ow", [D, D], F32, kind="ExternalInput").ap()
    w1 = nc.dram_tensor("w1", [F, D], F32, kind="ExternalInput").ap()
    w2 = nc.dram_tensor("w2", [D, F], F32, kind="ExternalInput").ap()
    consts_h = nc.dram_tensor("consts_hbm", [128, 8], F32, kind="ExternalInput").ap()
    ident_h = nc.dram_tensor("ident_hbm", [128, 128], BF16, kind="ExternalInput").ap()
    identf_h = nc.dram_tensor("identf_hbm", [128, 128], F32, kind="ExternalInput").ap()
    onescol_h = nc.dram_tensor("onescol_hbm", [128, 1], F32, kind="ExternalInput").ap()
    onesrow_h = nc.dram_tensor("onesrow_hbm", [1, 128], F32, kind="ExternalInput").ap()
    out_h = nc.dram_tensor("out", [TOK, D], F32, kind="ExternalOutput").ap()
    sp_qwt = nc.dram_tensor("sp_qwt", [16 * 16 * 128 * 128], BF16)
    sp_owt = nc.dram_tensor("sp_owt", [D * D], BF16)
    sp_w1t = nc.dram_tensor("sp_w1t", [16 * 16 * 128 * 512], BF16)
    sp_w2t = nc.dram_tensor("sp_w2t", [4 * 64 * 128 * 512], BF16)
    sp_xit = nc.dram_tensor("sp_xit", [4 * 16 * 4 * 128 * 128], BF16)

    from contextlib import ExitStack
    es = ExitStack()

    BASE = 16512
    TOPB = 229344
    _cur = [BASE]
    def at(name, shape, dt, off=None):
        if off is None:
            off = _cur[0]
            nbytes = shape[1] * {F32: 4, BF16: 2, I32: 4}[dt]
            _cur[0] = (off + nbytes + 63) // 64 * 64
            assert _cur[0] <= TOPB, (name, _cur[0])
        return nc.alloc_sbuf_tensor_at(name, shape, dt, offset=off)

    consts = at("consts", [128, 8], F32)
    ident = at("ident", [128, 128], BF16)
    onescol = at("onescol", [128, 1], F32)
    onesrow = at("onesrow", [1, 128], F32)
    ln_s = at("ln_s", [128, 6], F32)
    s1col = at("s1col", [128, 2], F32)
    smax1 = at("smax1", [128, NOWN], F32)
    mrow = at("mrow", [1, 4], F32)
    mbc = at("mbc", [128, 4], F32)
    thr = at("thr", [128, 4], F32)
    c1c2 = at("c1c2", [128, 10], F32)
    hmax = at("hmax", [128, 6], F32)
    mpart = at("mpart", [128, 64], F32)
    identf = at("identf", [128, 128], F32)
    rcp = at("rcp", [1, QG], F32)
    P_END = _cur[0]

    QT = at("QT", [128, 16 * TOK], BF16, off=P_END)
    KT = at("KT", [128, S], BF16, off=P_END + 65536)
    Vn = at("Vn", [128, NKT * 128], BF16, off=P_END + 73728)
    E1 = P_END + 81920
    eT = at("eT", [128, 2 * NKT * QG], BF16, off=E1)
    attnT = at("attnT", [128, H * QG], BF16, off=E1 + 32768)
    E3 = E1 + 40960
    wqa = at("wqa", [128, 2048], BF16, off=E1)
    wqb = at("wqb", [128, 2048], BF16, off=E1 + 4096)
    wqn = at("wqn", [128, 2048], BF16, off=E1 + 8192)
    wcb = [at("wcb0", [128, D], F32, off=E1 + 12288),
           at("wcb1", [128, D], F32, off=E1 + 20480)]
    stageW = [at("stageW0", [128, 2048], BF16, off=E1 + 28672),
              at("stageW1", [128, 2048], BF16, off=E1 + 32768)]
    o = E3
    xbuf = [at("xbuf0", [128, D], F32, off=o), at("xbuf1", [128, D], F32, off=o + 8192)]
    xnf = [at("xnf0", [128, D], F32, off=o), at("xnf1", [128, D], F32, off=o + 8192)]
    o += 16384
    xnb = at("xnb", [128, D], BF16, off=o); o += 4096
    kvb = at("kvb", [128, D], BF16, off=o); o += 4096
    xnt = at("xnt", [128, 16 * 512], BF16, off=o); o += 16384
    kwt = at("kwt", [128, 16 * 128], BF16, off=o); o += 4096
    vwt = at("vwt", [128, 16 * 128], BF16, off=o); o += 4096
    HQW = 1024
    qz = at("qz", [128, HQW], F32, off=o)
    qc = at("qc", [128, HQW], F32, off=o + 4096)
    xsq = at("xsq", [128, D], F32, off=o)
    o += 8192
    qi = at("qi", [128, HQW], I32, off=o)
    qf1 = at("qf1", [128, HQW], F32, off=o)
    qf2 = at("qf2", [128, HQW], F32, off=o + 4096)
    xn32 = at("xn32", [128, D], F32, off=o)
    o += 8192
    QRO = o
    qr = at("qr", [128, D], BF16, off=o); o += 4096
    stageX = at("stageX", [128, 2048], BF16, off=o); o += 4096
    stageV = at("stageV", [128, 512], BF16, off=o); o += 1024
    qwtb = [at("qwtb0", [128, 16 * 128], BF16, off=o), at("qwtb1", [128, 16 * 128], BF16, off=o + 4096)]
    o += 8192
    assert o <= TOPB, o
    owtb = [at("owtb0", [128, 512], BF16, off=QRO), at("owtb1", [128, 512], BF16, off=QRO + 1024)]
    xres = at("xres", [128, 2 * 512], F32, off=QRO + 2048)
    outp = [at("outp0", [128, 512], F32, off=QRO + 6144), at("outp1", [128, 512], F32, off=QRO + 8192)]
    rcpb = at("rcpb", [128, QG], F32, off=QRO + 10240)
    dsum = at("dsum", [128, QG], F32, off=QRO + 11264)

    h_sb = at("h_sb", [128, 4 * F], BF16, off=P_END)
    h16 = at("h16", [128, 4 * F], F16, off=P_END)
    hqT = at("hqT", [128, 64 * 512], BF16, off=P_END + 65536)
    f0 = P_END + 131072
    xitb = at("xitb", [128, 16 * 512], BF16, off=f0); f0 += 16384
    w1tb = [at("w1tb0", [128, 16 * 512], BF16, off=f0), at("w1tb1", [128, 16 * 512], BF16, off=f0 + 16384)]
    f0 += 32768
    w2tb = [at("w2tb0", [128, 4 * 512], BF16, off=f0), at("w2tb1", [128, 4 * 512], BF16, off=f0 + 4096)]
    f0 += 8192
    parb = [at(f"parb{i}", [128, 512], F32, off=f0 + i * 2048) for i in range(4)]
    f0 += 8192
    QW = 512
    fz = at("fz", [128, QW], F32, off=f0); f0 += 2048
    fi = at("fi", [128, QW], I32, off=f0); f0 += 2048
    fc_ = at("fc_", [128, QW], F32, off=f0); f0 += 2048
    ff1 = at("ff1", [128, QW], F32, off=f0); f0 += 2048
    ff2 = at("ff2", [128, QW], F32, off=f0); f0 += 2048
    fr1 = at("fr1", [128, QW], F32, off=f0); f0 += 2048
    assert f0 <= TOPB, f0

    PS0 = es.enter_context(nc.psum_tensor("PS0", [128, 1024], BF16))
    PS1 = es.enter_context(nc.psum_tensor("PS1", [128, 1024], BF16))
    PSf = [es.enter_context(nc.psum_tensor(f"PSf{i}", [128, 512], F32)) for i in range(6)]
    PS2, PS3, PS4, PS5, PS6, PS7 = PSf

    semnames = ("dX", "dW", "dSP", "dRD", "dRes", "dPar", "dOut", "T", "V", "Sx", "hz")
    sems = {n: Sem(es.enter_context(nc.semaphore(n))) for n in semnames}
    dX, dW, dSP, dRD, dRes, dPar, dOut, T, V, Sx, hz = (sems[n] for n in semnames)

    P = Builder()
    ap = bass.AP
    bu = {}

    def bwait(eng, key):
        if key in bu:
            sem, cnt = bu[key]
            P.wait(eng, sem, cnt)

    def cst(i):
        return consts[:, i:i + 1]

    def vh(fn):
        P.op("vector", fn, inc=hz)
        P.emit("vector", lambda e: e.drain()) if False else None
        P.streams["vector"].append(lambda e: e.drain())
        P.wait("vector", hz, hz.n)

    P.op("scalar", lambda e: e.dma_start(consts[:], consts_h[:]), inc=dX, k=16)
    P.op("scalar", lambda e: e.dma_start(ident[:], ident_h[:]), inc=dX, k=16)
    P.op("scalar", lambda e: e.dma_start(identf[:], identf_h[:]), inc=dX, k=16)
    P.op("scalar", lambda e: e.dma_start(onescol[:], onescol_h[:]), inc=dX, k=16)
    P.op("scalar", lambda e: e.dma_start(onesrow[:], onesrow_h[:]), inc=dX, k=16)
    P.op("scalar", lambda e: e.dma_start(xbuf[0][:], kw[:, :]), inc=dX, k=16)
    P.op("scalar", lambda e: e.dma_start(xbuf[1][:], vw[:, :]), inc=dX, k=16)
    P.wait("scalar", dX, dX.n)
    P.wait("vector", dX, dX.n)
    P.op("vector", lambda e: e.tensor_copy(xnb[:], xbuf[0][:]), inc=V)
    P.op("vector", lambda e: e.tensor_copy(kvb[:], xbuf[1][:]), inc=V)
    KV_CAST_V = V.n

    def emit_tile_transposes(src_sb, src_off_fn, dst_sb, dst_off_fn, ntiles,
                             gate_V=None, evac_dtype_note=None):
        for g in range((ntiles + 7) // 8):
            nb = min(8, ntiles - g * 8)
            pb = PS0 if g % 2 == 0 else PS1
            if gate_V:
                P.wait("tensor", V, gate_V)
            P.wait("tensor", V, V.n)
            def tfn(e, g=g, nb=nb, pb=pb):
                for j in range(nb - 1):
                    e.transpose(pb[:, j * 128:(j + 1) * 128],
                                src_sb[:, src_off_fn(g * 8 + j):src_off_fn(g * 8 + j) + 128],
                                ident[:])
                return e.transpose(pb[:, (nb - 1) * 128:nb * 128],
                                   src_sb[:, src_off_fn(g * 8 + nb - 1):src_off_fn(g * 8 + nb - 1) + 128],
                                   ident[:])
            P.op("tensor", tfn, inc=T)
            P.wait("vector", T, T.n)
            def vfn(e, g=g, nb=nb, pb=pb):
                for j in range(nb - 1):
                    e.tensor_copy(dst_sb[:, dst_off_fn(g * 8 + j):dst_off_fn(g * 8 + j) + 128],
                                  pb[:, j * 128:(j + 1) * 128])
                return e.tensor_copy(dst_sb[:, dst_off_fn(g * 8 + nb - 1):dst_off_fn(g * 8 + nb - 1) + 128],
                                     pb[:, (nb - 1) * 128:nb * 128])
            P.op("vector", vfn, inc=V)
            P.wait("vector", V, V.n)
        return V.n

    emit_tile_transposes(xnb, lambda t: t * 128, kwt, lambda t: t * 128, 16,
                         gate_V=KV_CAST_V)
    emit_tile_transposes(kvb, lambda t: t * 128, vwt, lambda t: t * 128, 16,
                         gate_V=KV_CAST_V)
    KVW_V = V.n

    cast_done = {}
    spill_done = {}

    def emit_co_spill(src_ap, kind):
        for ch in range(16):
            b = wcb[ch % 2]
            bwait("sync", ("wcb", ch % 2))
            P.op("sync", lambda e, ch=ch, b=b, src=src_ap:
                 e.dma_start(b[:], src[ch * 128:(ch + 1) * 128, :]), inc=dW, k=16)
            P.wait("vector", dW, dW.n)
            st = stageW[ch % 2]
            bwait("vector", ("stageW", ch % 2))
            P.op("vector", lambda e, b=b, st=st: e.tensor_copy(st[:], b[:]), inc=V)
            bu[("wcb", ch % 2)] = (V, V.n)
            cv = V.n
            emit_tile_transposes(st, lambda t: t * 128, wqn, lambda t: t * 128, 16,
                                 gate_V=cv)
            P.wait("scalar", V, V.n)
            if kind == "q":
                dst = ap(sp_qwt, ch * 16 * 128 * 128,
                         [[128, 128], [128 * 128, 16], [1, 128]])
            else:
                dst = ap(sp_owt, ch * 128,
                         [[D, 128], [128 * D, 16], [1, 128]])
            src_st = ap(wqn, 0, [[2048, 128], [128, 16], [1, 128]])
            P.op("scalar", lambda e, dst=dst, src_st=src_st:
                 e.dma_start(dst, src_st), inc=dSP, k=16)
            bu[("stageW", ch % 2)] = (dSP, dSP.n)
            bu["wqn"] = (dSP, dSP.n)
            P.wait("vector", dSP, dSP.n)

    emit_co_spill(qw, "q")
    QWT_SPILLED = dSP.n
    emit_co_spill(ow, "o")
    OWT_SPILLED = dSP.n


    def emit_mpass(src_ap, nchunks, acc_col, w2mode=False):
        for ch in range(nchunks):
            b = wcb[ch % 2]
            bwait("sync", ("wcb", ch % 2))
            if w2mode:
                r, fs = ch // 4, ch % 4
                srcap = src_ap[r * 128:(r + 1) * 128, fs * 2048:(fs + 1) * 2048]
            else:
                srcap = src_ap[ch * 128:(ch + 1) * 128, :]
            P.op("sync", lambda e, b=b, srcap=srcap: e.dma_start(b[:], srcap), inc=dW, k=16)
            P.wait("vector", dW, dW.n)
            P.op("vector", lambda e, b=b, ch=ch:
                 e.tensor_reduce(mpart[:, ch:ch + 1], b[:], AX.X, ALU.add,
                                 apply_absolute_value=True), inc=V)
            bu[("wcb", ch % 2)] = (V, V.n)
        P.wait("vector", V, V.n)
        n = 64
        while n > 1:
            h = n // 2
            vh(lambda e, h=h: e.tensor_add(mpart[:, 0:h], mpart[:, 0:h], mpart[:, h:2 * h]))
            n = h
        P.wait("tensor", hz, hz.n)
        P.op("tensor", lambda e: e.transpose(PS6[0:1, 0:128], mpart[:, 0:1], identf[:]), inc=T)
        P.wait("vector", T, T.n)
        vh(lambda e: e.tensor_copy(mpart[0:1, 0:64], PS6[0:1, 0:64]))
        vh(lambda e: e.tensor_add(mpart[0:1, 0:64], mpart[0:1, 0:64], PS6[0:1, 64:128]))
        n = 64
        while n > 1:
            h = n // 2
            vh(lambda e, h=h: e.tensor_add(mpart[0:1, 0:h], mpart[0:1, 0:h], mpart[0:1, h:2 * h]))
            n = h
        P.op("vector", lambda e, acc_col=acc_col:
             e.tensor_scalar(out=mrow[0:1, acc_col:acc_col + 1], in0=mpart[0:1, 0:1],
                             scalar1=1.0 / (F * D), scalar2=1e-5,
                             op0=ALU.mult, op1=ALU.max), inc=V)
        P.wait("vector", V, V.n)

    emit_mpass(w1, 64, 0)
    emit_mpass(w2, 64, 1, w2mode=True)
    P.wait("tensor", V, V.n)
    P.op("tensor", lambda e: e.matmul(PS6[:, 4:6], onesrow[:], mrow[0:1, 0:2],
                                      start=True, stop=True), inc=T)
    P.wait("vector", T, T.n)
    vh(lambda e: e.tensor_copy(mbc[:, 0:2], PS6[:, 4:6]))
    vh(lambda e: e.tensor_scalar(out=thr[:, 0:1], in0=mbc[:, 0:1], scalar1=0.5,
                                 scalar2=None, op0=ALU.mult))
    vh(lambda e: e.tensor_scalar(out=thr[:, 1:2], in0=mbc[:, 0:1], scalar1=-0.5,
                                 scalar2=None, op0=ALU.mult))
    vh(lambda e: e.tensor_scalar(out=thr[:, 2:3], in0=mbc[:, 1:2], scalar1=0.5,
                                 scalar2=None, op0=ALU.mult))
    P.op("vector", lambda e: e.tensor_scalar(out=thr[:, 3:4], in0=mbc[:, 1:2],
                                             scalar1=-0.5, scalar2=None,
                                             op0=ALU.mult), inc=V)
    THR_V = V.n

    def emit_wquant(src_ap, nchunks, thr_i, which):
        for ch in range(nchunks):
            b = wcb[ch % 2]
            bwait("sync", ("wcb", ch % 2))
            if which == "w2":
                r, fs = ch // 4, ch % 4
                srcap = src_ap[r * 128:(r + 1) * 128, fs * 2048:(fs + 1) * 2048]
            else:
                srcap = src_ap[ch * 128:(ch + 1) * 128, :]
            P.op("sync", lambda e, b=b, srcap=srcap: e.dma_start(b[:], srcap), inc=dW, k=16)
            P.wait("vector", dW, dW.n)
            P.wait("vector", V, THR_V)
            bwait("vector", "wqn")
            vh(lambda e, b=b, thr_i=thr_i:
               e.tensor_single_scalar(wqa[:], b[:], thr[:, thr_i:thr_i + 1], ALU.is_gt))
            vh(lambda e, b=b, thr_i=thr_i:
               e.tensor_single_scalar(wqb[:], b[:], thr[:, thr_i + 1:thr_i + 2], ALU.is_lt))
            P.op("vector", lambda e: e.tensor_sub(wqn[:], wqa[:], wqb[:]), inc=V)
            bu[("wcb", ch % 2)] = (V, V.n)
            P.wait("vector", V, V.n)
            st = stageW[ch % 2]
            bwait("vector", ("stageW", ch % 2))
            emit_tile_transposes(wqn, lambda t: t * 128, st, lambda t: t * 128, 16)
            bu["wqn"] = (T, T.n)
            P.wait("scalar", V, V.n)
            if which == "w1":
                f0 = ch
                base = (f0 // 4) * (128 * 512) + (f0 % 4) * 128
                dst = ap(sp_w1t, base, [[512, 128], [16 * 128 * 512, 16], [1, 128]])
            else:
                r, fs = ch // 4, ch % 4
                base = (r // 4) * (64 * 128 * 512) + fs * 16 * (128 * 512) + (r % 4) * 128
                dst = ap(sp_w2t, base, [[512, 128], [128 * 512, 16], [1, 128]])
            src_st = ap(st, 0, [[2048, 128], [128, 16], [1, 128]])
            P.op("scalar", lambda e, dst=dst, src_st=src_st:
                 e.dma_start(dst, src_st), inc=dSP, k=16)
            bu[("stageW", ch % 2)] = (dSP, dSP.n)

    emit_wquant(w1, 64, 0, "w1")
    W1_SPILLED = dSP.n
    emit_wquant(w2, 64, 2, "w2")
    W2_SPILLED = dSP.n

    xn_done = {}
    for tg in range(8):
        for j in range(4):
            c = tg * 4 + j
            own = c < NOWN
            buf = xbuf[c % 2]
            if c >= 2:
                P.wait("scalar", V, xn_done[c - 2])
            else:
                P.wait("scalar", V, KVW_V)
            P.op("scalar", lambda e, c=c, buf=buf:
                 e.dma_start(buf[:], xb[c * 128:(c + 1) * 128, :]), inc=dX, k=16)
            P.wait("vector", dX, dX.n)
            vh(lambda e, buf=buf: e.tensor_reduce(ln_s[:, 0:1], buf[:], AX.X, ALU.add))
            vh(lambda e, buf=buf:
               e.scalar_tensor_tensor(out=xsq[:], in0=buf[:], scalar=1.0, in1=buf[:],
                                      op0=ALU.mult, op1=ALU.mult,
                                      accum_out=ln_s[:, 1:2]))
            vh(lambda e: e.tensor_scalar(out=ln_s[:, 2:3], in0=ln_s[:, 0:1],
                                         scalar1=1.0 / D, scalar2=None, op0=ALU.mult))
            vh(lambda e: e.tensor_scalar(out=ln_s[:, 3:4], in0=ln_s[:, 1:2],
                                         scalar1=1.0 / D, scalar2=None, op0=ALU.mult))
            vh(lambda e: e.tensor_mul(ln_s[:, 4:5], ln_s[:, 2:3], ln_s[:, 2:3]))
            vh(lambda e: e.tensor_sub(ln_s[:, 4:5], ln_s[:, 3:4], ln_s[:, 4:5]))
            vh(lambda e: e.tensor_scalar(out=ln_s[:, 4:5], in0=ln_s[:, 4:5],
                                         scalar1=EPS, scalar2=None, op0=ALU.add))
            P.op("vector", lambda e: e.reciprocal(ln_s[:, 5:6], ln_s[:, 4:5]), inc=V)
            P.wait("scalar", V, V.n)
            P.op("scalar", lambda e: e.activation(ln_s[:, 5:6], ln_s[:, 5:6], ACT.Sqrt),
                 inc=Sx)
            P.wait("vector", Sx, Sx.n)
            vh(lambda e, buf=buf: e.tensor_scalar(out=xn32[:], in0=buf[:],
                                                  scalar1=ln_s[:, 2:3], scalar2=None,
                                                  op0=ALU.subtract))
            xf = xnf[c % 2]
            vh(lambda e, xf=xf: e.tensor_scalar(out=xf[:], in0=xn32[:],
                                                scalar1=ln_s[:, 5:6], scalar2=None,
                                                op0=ALU.mult))
            bwait("vector", "xnb")
            P.op("vector", lambda e, xf=xf: e.tensor_copy(xnb[:], xf[:]), inc=V)
            xn_V = V.n
            P.wait("vector", V, xn_V)

            if own:
                oi = c
                vh(lambda e, oi=oi, xf=xf: e.tensor_reduce(smax1[:, oi:oi + 1], xf[:], AX.X,
                                                    ALU.max, apply_absolute_value=True))
                vh(lambda e, oi=oi: e.tensor_scalar(out=smax1[:, oi:oi + 1],
                                                    in0=smax1[:, oi:oi + 1], scalar1=1e-5,
                                                    scalar2=None, op0=ALU.max))
                vh(lambda e, oi=oi: e.reciprocal(s1col[:, 0:1], smax1[:, oi:oi + 1]))
                vh(lambda e: e.tensor_scalar(out=s1col[:, 0:1], in0=s1col[:, 0:1],
                                             scalar1=127.0, scalar2=None, op0=ALU.mult))
                bwait("vector", "qr")
                for hh in range(D // HQW):
                    s0 = hh * HQW
                    vh(lambda e, s0=s0, xf=xf: e.tensor_scalar(out=qz[:], in0=xf[:, s0:s0 + HQW],
                                                        scalar1=s1col[:, 0:1], scalar2=None,
                                                        op0=ALU.mult))
                    vh(lambda e: e.tensor_copy(qi[:], qz[:]))
                    vh(lambda e: e.tensor_copy(qc[:], qi[:]))
                    vh(lambda e: e.scalar_tensor_tensor(out=qf1[:], in0=qz[:], scalar=-0.5,
                                                        in1=qc[:], op0=ALU.add, op1=ALU.is_ge))
                    vh(lambda e: e.scalar_tensor_tensor(out=qf2[:], in0=qz[:], scalar=0.5,
                                                        in1=qc[:], op0=ALU.add, op1=ALU.is_lt))
                    vh(lambda e: e.tensor_add(qf1[:], qc[:], qf1[:]))
                    vh(lambda e, s0=s0: e.tensor_sub(qr[:, s0:s0 + HQW], qf1[:], qf2[:]))
                P.op("vector", lambda e: e.engine_nop(), inc=V)
                xi_V = V.n
                P.wait("vector", V, xi_V)

            emit_tile_transposes(xnb, lambda t: t * 128,
                                 xnt, lambda t, j=j: t * 512 + j * 128, 16,
                                 gate_V=xn_V)
            bu["xnb"] = (T, T.n)
            if own:
                emit_tile_transposes(qr, lambda t: t * 128,
                                     stageX, lambda t: t * 128, 16, gate_V=xi_V)
                bu["qr"] = (T, T.n)
                qtr, cc = c // 4, c % 4
                base = qtr * (16 * 4 * 128 * 128) + cc * 128
                dst = ap(sp_xit, base, [[4 * 128, 128], [4 * 128 * 128, 16], [1, 128]])
                src_st = ap(stageX, 0, [[2048, 128], [128, 16], [1, 128]])
                P.wait("scalar", V, V.n)
                P.op("scalar", lambda e, dst=dst, src_st=src_st:
                     e.dma_start(dst, src_st), inc=dSP, k=16)
                P.wait("vector", dSP, dSP.n)
            xn_done[c] = V.n

        P.wait("tensor", V, V.n)
        def t_k(e, tg=tg):
            for dt in range(15):
                e.matmul(PS4[:, 0:512], kwt[:, dt * 128:(dt + 1) * 128],
                         xnt[:, dt * 512:(dt + 1) * 512], start=(dt == 0), stop=False)
            return e.matmul(PS4[:, 0:512], kwt[:, 15 * 128:16 * 128],
                            xnt[:, 15 * 512:16 * 512], start=False, stop=True)
        P.op("tensor", t_k, inc=T)
        def t_v(e, tg=tg):
            for dt in range(15):
                e.matmul(PS5[:, 0:512], vwt[:, dt * 128:(dt + 1) * 128],
                         xnt[:, dt * 512:(dt + 1) * 512], start=(dt == 0), stop=False)
            return e.matmul(PS5[:, 0:512], vwt[:, 15 * 128:16 * 128],
                            xnt[:, 15 * 512:16 * 512], start=False, stop=True)
        P.op("tensor", t_v, inc=T)
        P.wait("vector", T, T.n)
        P.op("vector", lambda e, tg=tg:
             e.tensor_copy(KT[:, tg * 512:(tg + 1) * 512], PS4[:, 0:512]), inc=V)
        P.op("vector", lambda e: e.tensor_copy(stageV[:, 0:512], PS5[:, 0:512]), inc=V)
        vt_V = V.n
        emit_tile_transposes(stageV, lambda t: t * 128,
                             Vn, lambda t, tg=tg: (tg * 4 + t) * 128, 4, gate_V=vt_V)
        P.wait("tensor", V, V.n)

        if tg < 4:
            for do in range(16):
                qb = qwtb[do % 2]
                bwait("sync", ("qwtb", do % 2))
                P.wait("sync", dSP, QWT_SPILLED)
                src = ap(sp_qwt, do * 16 * 128 * 128, [[128, 128], [128 * 128, 16], [1, 128]])
                P.op("sync", lambda e, qb=qb, src=src: e.dma_start(qb[:], src),
                     inc=dRD, k=16)
                P.wait("tensor", dRD, dRD.n)
                pbank = PS2 if do % 2 == 0 else PS3
                def t_q(e, qb=qb, pbank=pbank):
                    for dt in range(15):
                        e.matmul(pbank[:, 0:512], qb[:, dt * 128:(dt + 1) * 128],
                                 xnt[:, dt * 512:(dt + 1) * 512], start=(dt == 0),
                                 stop=False)
                    return e.matmul(pbank[:, 0:512], qb[:, 15 * 128:16 * 128],
                                    xnt[:, 15 * 512:16 * 512], start=False, stop=True)
                P.op("tensor", t_q, inc=T)
                bu[("qwtb", do % 2)] = (T, T.n)
                P.wait("vector", T, T.n)
                P.op("vector", lambda e, tg=tg, do=do, pbank=pbank:
                     e.tensor_copy(QT[:, do * TOK + tg * 512: do * TOK + (tg + 1) * 512],
                                   pbank[:, 0:512]), inc=V)
                P.wait("tensor", V, V.n)

    PROLOGUE_V = V.n

    P.wait("tensor", V, PROLOGUE_V)
    P.wait("scalar", V, PROLOGUE_V)
    sbank_user = {}
    pv_user = {}
    et_user = {}
    oproj_T = {}
    oproj_done_V = 0
    oev_prev = 0
    for qg in range(NQG):

        for h in range(H):
            hp = h % 2
            eoff = hp * NKT * QG
            pvbank = PS4 if hp == 0 else PS5
            if hp in pv_user:
                P.wait("tensor", V, pv_user[hp])
            if hp in et_user:
                P.wait("scalar", V, et_user[hp])
            for kt in range(NKT):
                sbank = PS2 if kt % 2 == 0 else PS3
                if kt % 2 in sbank_user:
                    P.wait("tensor", Sx, sbank_user[kt % 2])
                def t_sc(e, h=h, qg=qg, kt=kt, sbank=sbank):
                    return e.matmul(sbank[:, 0:QG], KT[:, kt * 128:(kt + 1) * 128],
                                    QT[:, h * TOK + qg * QG: h * TOK + (qg + 1) * QG],
                                    start=True, stop=True)
                P.op("tensor", t_sc, inc=T)
                P.wait("scalar", T, T.n)
                def s_exp(e, kt=kt, sbank=sbank, eoff=eoff):
                    return e.activation(eT[:, eoff + kt * QG: eoff + (kt + 1) * QG],
                                        sbank[:, 0:QG], ACT.Exp,
                                        bias=cst(C_SHIFT), scale=cst(C_SCALE))
                P.op("scalar", s_exp, inc=Sx)
                sbank_user[kt % 2] = Sx.n
                P.wait("tensor", Sx, Sx.n)
                def t_pv(e, kt=kt, pvbank=pvbank, eoff=eoff):
                    return e.matmul(pvbank[:, 0:QG], Vn[:, kt * 128:(kt + 1) * 128],
                                    eT[:, eoff + kt * QG: eoff + (kt + 1) * QG],
                                    start=(kt == 0), stop=(kt == NKT - 1))
                P.op("tensor", t_pv, inc=(T if kt == NKT - 1 else None))
            pv_T = T.n
            P.wait("vector", Sx, Sx.n)
            def v_dn(e, eoff=eoff):
                src = ap(eT, eoff, [[2 * NKT * QG, 128], [1, QG], [QG, NKT]])
                return e.tensor_reduce(dsum[:], src, AX.X, ALU.add)
            P.op("vector", v_dn, inc=V)
            et_user[hp] = V.n
            P.wait("tensor", V, V.n)
            P.op("tensor", lambda e: e.matmul(PS6[0:1, 0:QG], onescol[:], dsum[:],
                                              start=True, stop=True), inc=T)
            P.wait("vector", T, T.n)
            P.op("vector", lambda e: e.reciprocal(rcp[0:1, :], PS6[0:1, 0:QG]), inc=V)
            P.wait("tensor", V, V.n)
            P.op("tensor", lambda e: e.matmul(PS6[:, 0:QG], onesrow[:], rcp[0:1, :],
                                              start=True, stop=True), inc=T)
            P.wait("vector", T, max(T.n, pv_T))
            vh(lambda e: e.tensor_copy(rcpb[:], PS6[:, 0:QG]))
            def v_nrm(e, h=h, pvbank=pvbank):
                return e.tensor_mul(attnT[:, h * QG:(h + 1) * QG], pvbank[:, 0:QG],
                                    rcpb[:])
            P.op("vector", v_nrm, inc=V)
            pv_user[hp] = V.n

        heads_V = V.n
        for dc in range(4):
            if oproj_done_V or dc > 0:
                P.wait("scalar", V, oev_prev)
            for tt in range(2):
                P.op("scalar", lambda e, qg=qg, tt=tt, dc=dc:
                     e.dma_start(xres[:, tt * 512:(tt + 1) * 512],
                                 xb[qg * 256 + tt * 128: qg * 256 + (tt + 1) * 128,
                                    dc * 512:(dc + 1) * 512]),
                     inc=dRes, k=16)
            xres_d = dRes.n
            for hd in range(16):
                obuf = owtb[hd % 2]
                bwait("sync", ("owtb", hd % 2))
                P.wait("sync", dSP, OWT_SPILLED)
                src = ap(sp_owt, hd * 128 * D + dc * 512, [[D, 128], [1, 512]])
                P.op("sync", lambda e, obuf=obuf, src=src: e.dma_start(obuf[:], src),
                     inc=dRD, k=16)
                P.wait("tensor", dRD, dRD.n)
                if hd == 0:
                    P.wait("tensor", V, heads_V)
                def t_op(e, hd=hd, obuf=obuf):
                    e.matmul(PS4[:, 0:512],
                             attnT[:, hd * QG: hd * QG + 128], obuf[:],
                             start=(hd == 0), stop=(hd == 15))
                    return e.matmul(PS5[:, 0:512],
                                    attnT[:, hd * QG + 128: hd * QG + 256], obuf[:],
                                    start=(hd == 0), stop=(hd == 15))
                P.op("tensor", t_op, inc=T)
                bu[("owtb", hd % 2)] = (T, T.n)
            P.wait("vector", T, T.n)
            P.wait("vector", dRes, xres_d)
            if dc > 0:
                P.wait("vector", dPar, dPar.n)
            for tt in range(2):
                P.op("vector", lambda e, tt=tt:
                     e.tensor_add(outp[tt][:], (PS4 if tt == 0 else PS5)[:, 0:512],
                                  xres[:, tt * 512:(tt + 1) * 512]),
                     inc=V)
            oev_V = V.n
            oev_prev = oev_V
            P.wait("tensor", V, oev_V)
            P.wait("scalar", V, oev_V)
            for tt in range(2):
                P.op("scalar", lambda e, qg=qg, tt=tt, dc=dc:
                     e.dma_start(out_h[qg * 256 + tt * 128: qg * 256 + (tt + 1) * 128,
                                       dc * 512:(dc + 1) * 512], outp[tt][:]),
                     inc=dPar, k=16)
        oproj_done_V = V.n

    ATTN_PAR = dPar.n

    ffn1_T = {}
    ffn1_S = {}
    ffn2_T = {}
    ffn2_ev_V = 0
    xit_T_prev = 0
    for q in range(4):
        P.wait("sync", dSP, dSP.n)
        P.wait("sync", T, xit_T_prev)
        src = ap(sp_xit, q * (16 * 4 * 128 * 128), [[4 * 128, 128], [4 * 128 * 128, 16], [1, 4 * 128]])
        P.op("sync", lambda e, src=src: e.dma_start(xitb[:], src), inc=dRD, k=16)
        xit_rd = dRD.n
        P.wait("vector", V, PROLOGUE_V)
        for tt in range(4):
            P.op("vector", lambda e, q=q, tt=tt:
                 e.tensor_scalar(out=c1c2[:, tt:tt + 1],
                                 in0=smax1[:, q * 4 + tt:q * 4 + tt + 1],
                                 scalar1=mbc[:, 0:1], scalar2=1.0 / 127.0,
                                 op0=ALU.mult, op1=ALU.mult),
                 inc=(V if tt == 3 else None))
        c1_V = V.n

        for fcx in range(16):
            wb = w1tb[fcx % 2]
            bwait("sync", ("w1tb", fcx % 2))
            P.wait("sync", dSP, W1_SPILLED)
            src = ap(sp_w1t, fcx * 128 * 512, [[512, 128], [16 * 128 * 512, 16], [1, 512]])
            P.op("sync", lambda e, wb=wb, src=src: e.dma_start(wb[:], src), inc=dRD, k=16)
            P.wait("tensor", dRD, max(dRD.n, xit_rd))
            for tt in range(4):
                pbank = PS2 if (fcx * 4 + tt) % 2 == 0 else PS3
                key = (q, fcx * 4 + tt - 2)
                if key in ffn1_S:
                    P.wait("tensor", Sx, ffn1_S[key])
                def t_f1(e, tt=tt, wb=wb, pbank=pbank):
                    for dt in range(15):
                        e.matmul(pbank[:, 0:512],
                                 xitb[:, dt * 512 + tt * 128: dt * 512 + tt * 128 + 128],
                                 wb[:, dt * 512:(dt + 1) * 512],
                                 start=(dt == 0), stop=False)
                    return e.matmul(pbank[:, 0:512],
                                    xitb[:, 15 * 512 + tt * 128: 15 * 512 + tt * 128 + 128],
                                    wb[:, 15 * 512:16 * 512], start=False, stop=True)
                P.op("tensor", t_f1, inc=T)
                ffn1_T[(q, fcx)] = T.n
                bu[("w1tb", fcx % 2)] = (T, T.n)
                P.wait("scalar", T, T.n)
                P.wait("scalar", V, c1_V)
                def s_silu(e, tt=tt, fcx=fcx, pbank=pbank):
                    return e.activation(h16[:, tt * F + fcx * 512: tt * F + fcx * 512 + 512],
                                        pbank[:, 0:512], ACT.Silu, scale=c1c2[:, tt:tt + 1])
                P.op("scalar", s_silu, inc=Sx)
                ffn1_S[(q, fcx * 4 + tt)] = Sx.n
        ffn1_done_S = Sx.n

        P.wait("vector", Sx, ffn1_done_S)
        for tt in range(4):
            vh(lambda e, tt=tt: e.tensor_reduce(hmax[:, 0:1], h16[:, tt * F:(tt + 1) * F],
                                                AX.X, ALU.max, apply_absolute_value=True))
            vh(lambda e: e.tensor_scalar(out=hmax[:, 1:2], in0=hmax[:, 0:1], scalar1=1e-5,
                                         scalar2=None, op0=ALU.max))
            vh(lambda e: e.reciprocal(hmax[:, 2:3], hmax[:, 1:2]))
            vh(lambda e: e.tensor_scalar(out=hmax[:, 2:3], in0=hmax[:, 2:3], scalar1=127.0,
                                         scalar2=None, op0=ALU.mult))
            vh(lambda e, tt=tt: e.tensor_scalar(out=c1c2[:, 4 + tt:5 + tt], in0=hmax[:, 1:2],
                                                scalar1=mbc[:, 1:2], scalar2=1.0 / 127.0,
                                                op0=ALU.mult, op1=ALU.mult))
            for cq in range(F // QW):
                sl0 = tt * F + cq * QW
                vh(lambda e, sl0=sl0: e.tensor_scalar(out=fz[:], in0=h16[:, sl0:sl0 + QW],
                                                      scalar1=hmax[:, 2:3], scalar2=None,
                                                      op0=ALU.mult))
                vh(lambda e: e.tensor_copy(fi[:], fz[:]))
                vh(lambda e: e.tensor_copy(fc_[:], fi[:]))
                vh(lambda e: e.scalar_tensor_tensor(out=ff1[:], in0=fz[:], scalar=-0.5,
                                                    in1=fc_[:], op0=ALU.add, op1=ALU.is_ge))
                vh(lambda e: e.scalar_tensor_tensor(out=ff2[:], in0=fz[:], scalar=0.5,
                                                    in1=fc_[:], op0=ALU.add, op1=ALU.is_lt))
                vh(lambda e: e.tensor_add(fr1[:], fc_[:], ff1[:]))
                vh(lambda e, sl0=sl0: e.tensor_sub(h_sb[:, sl0:sl0 + QW], fr1[:], ff2[:]))
            P.op("vector", lambda e: e.engine_nop(), inc=V)
            hq_V = V.n
            emit_tile_transposes(h_sb, lambda t, tt=tt: tt * F + t * 128,
                                 hqT, lambda t, tt=tt: t * 512 + tt * 128, 64,
                                 gate_V=hq_V)
        hqT_V = V.n

        for dq in range(4):
            P.wait("scalar", dPar, ATTN_PAR)
            if ffn2_ev_V:
                P.wait("scalar", V, ffn2_ev_V)
            for tt in range(4):
                P.op("scalar", lambda e, q=q, tt=tt, dq=dq:
                     e.dma_start(parb[tt][:],
                                 out_h[q * 512 + tt * 128: q * 512 + (tt + 1) * 128,
                                       dq * 512:(dq + 1) * 512]), inc=dRes, k=16)
            par_d = dRes.n
            for grp in range(16):
                wb2 = w2tb[grp % 2]
                bwait("sync", ("w2tb", grp % 2))
                P.wait("sync", dSP, W2_SPILLED)
                src = ap(sp_w2t, dq * (64 * 128 * 512) + grp * 4 * (128 * 512),
                         [[512, 128], [128 * 512, 4], [1, 512]])
                P.op("sync", lambda e, wb2=wb2, src=src: e.dma_start(wb2[:], src),
                     inc=dRD, k=16)
                P.wait("tensor", dRD, dRD.n)
                if grp == 0:
                    P.wait("tensor", V, hqT_V)
                    if ffn2_ev_V:
                        P.wait("tensor", V, ffn2_ev_V)
                def t_f2(e, grp=grp, wb2=wb2):
                    last = None
                    for j in range(4):
                        ft = grp * 4 + j
                        for tt in range(4):
                            last = e.matmul(PSf[2 + tt][:, 0:512],
                                            hqT[:, ft * 512 + tt * 128: ft * 512 + tt * 128 + 128],
                                            wb2[:, j * 512:(j + 1) * 512],
                                            start=(ft == 0), stop=(ft == 63))
                    return last
                P.op("tensor", t_f2, inc=T)
                ffn2_T[(q, dq, grp)] = T.n
                bu[("w2tb", grp % 2)] = (T, T.n)
            P.wait("vector", T, T.n)
            P.wait("vector", dRes, par_d)
            for tt in range(4):
                P.op("vector", lambda e, tt=tt:
                     e.scalar_tensor_tensor(out=parb[tt][:], in0=PSf[2 + tt][:, 0:512],
                                            scalar=c1c2[:, 4 + tt:5 + tt],
                                            in1=parb[tt][:],
                                            op0=ALU.mult, op1=ALU.add),
                     inc=(V if tt == 3 else None))
            ffn2_ev_V = V.n
            P.wait("scalar", V, ffn2_ev_V)
            for tt in range(4):
                P.op("scalar", lambda e, q=q, tt=tt, dq=dq:
                     e.dma_start(out_h[q * 512 + tt * 128: q * 512 + (tt + 1) * 128,
                                       dq * 512:(dq + 1) * 512], parb[tt][:]),
                     inc=dOut, k=16)
        xit_T_prev = ffn1_T[(q, 15)]

    P.wait("sync", dOut, dOut.n)
    P.wait("sync", dPar, dPar.n)
    P.wait("sync", dSP, dSP.n)

    with es, nc.Block() as block:
        @block.sync
        def _(sync):
            for fn in P.streams["sync"]:
                fn(sync)

        @block.scalar
        def _(scalar):
            for fn in P.streams["scalar"]:
                fn(scalar)

        @block.vector
        def _(vector):
            for fn in P.streams["vector"]:
                fn(vector)

        @block.tensor
        def _(tensor):
            for fn in P.streams["tensor"]:
                fn(tensor)

    return nc


def kernel(L0_attn_x, L0_attn_q_w, L0_attn_k_w, L0_attn_v_w, L0_attn_o_w,
           L0_ffn_W1, L0_ffn_W2):
    x = np.ascontiguousarray(np.asarray(L0_attn_x, dtype=np.float32))
    qwa = np.ascontiguousarray(np.asarray(L0_attn_q_w, dtype=np.float32))
    kwa = np.ascontiguousarray(np.asarray(L0_attn_k_w, dtype=np.float32))
    vwa = np.ascontiguousarray(np.asarray(L0_attn_v_w, dtype=np.float32))
    owa = np.ascontiguousarray(np.asarray(L0_attn_o_w, dtype=np.float32))
    w1a = np.ascontiguousarray(np.asarray(L0_ffn_W1, dtype=np.float32))
    w2a = np.ascontiguousarray(np.asarray(L0_ffn_W2, dtype=np.float32))

    consts = np.zeros((128, 8), dtype=np.float32)
    consts[:, C_ONE] = 1.0
    consts[:, C_SCALE] = SCALE
    consts[:, C_SHIFT] = SHIFT
    consts[:, C_IMEAN] = 1.0 / D
    consts[:, C_127] = 127.0
    consts[:, C_HALF] = 0.5
    consts[:, C_NHALF] = -0.5
    consts[:, C_EPS] = EPS
    ident = np.eye(128, dtype=ml_dtypes.bfloat16)
    identf = np.eye(128, dtype=np.float32)
    onescol = np.ones((128, 1), dtype=np.float32)
    onesrow = np.ones((1, 128), dtype=np.float32)

    nc = build_core()

    in_maps = []
    for c in range(8):
        b, hf = c // 2, c % 2
        xbc = np.concatenate([x[b, hf * TOK:(hf + 1) * TOK, :],
                              x[b, (1 - hf) * TOK:(2 - hf) * TOK, :]], axis=0)
        in_maps.append({
            "xb": np.ascontiguousarray(xbc), "qw": qwa, "kw": kwa, "vw": vwa,
            "ow": owa, "w1": w1a, "w2": w2a, "consts_hbm": consts, "ident_hbm": ident,
            "onescol_hbm": onescol, "onesrow_hbm": onesrow, "identf_hbm": identf,
        })

    res = run_bass_kernel_spmd(nc, in_maps, core_ids=list(range(8)))

    out = np.empty((B, S, D), dtype=np.float32)
    for c in range(8):
        b, hf = c // 2, c % 2
        out[b, hf * TOK:(hf + 1) * TOK, :] = res.results[c]["out"]
    return out


# version 22 (final)
# speedup vs baseline: 1.3385x; 1.3385x over previous
import numpy as np
import ml_dtypes
import concourse.bass as bass
import concourse.mybir as mybir
from concourse.bass_utils import run_bass_kernel_spmd

F32 = mybir.dt.float32
F16 = mybir.dt.float16
BF16 = mybir.dt.bfloat16
I32 = mybir.dt.int32
ALU = mybir.AluOpType
ACT = mybir.ActivationFunctionType
AX = mybir.AxisListType

B, S, D, H, DH, F = 4, 4096, 2048, 16, 128, 8192
TOK = 2048
SCALE = 1.0 / float(np.sqrt(DH))
SHIFT = -30.0
EPS = 1e-5
NCHUNK = S // 128
NOWN = TOK // 128
QG = 256
NQG = TOK // QG
NKT = S // 128
C_ONE, C_SCALE, C_SHIFT, C_IMEAN, C_127, C_HALF, C_NHALF, C_EPS = range(8)


class Sem:
    def __init__(self, handle):
        self.h = handle
        self.n = 0


class Builder:

    def __init__(self):
        self.streams = {k: [] for k in ("sync", "scalar", "vector", "tensor", "gpsimd")}

    def op(self, eng, fn, inc=None, k=1):
        if inc is None:
            self.streams[eng].append(fn)
            return None
        def wrapped(e, fn=fn, sem=inc, k=k):
            fn(e).then_inc(sem.h, k)
        self.streams[eng].append(wrapped)
        inc.n += k
        return inc.n

    def wait(self, eng, sem, val):
        if val and val > 0:
            self.streams[eng].append(lambda e, s=sem, v=val: e.wait_ge(s.h, v))


def build_core():
    nc = bass.Bass("TRN2", target_bir_lowering=False, debug=False)

    xb = nc.dram_tensor("xb", [S, D], F32, kind="ExternalInput").ap()
    qw = nc.dram_tensor("qw", [D, D], F32, kind="ExternalInput").ap()
    kw = nc.dram_tensor("kw", [DH, D], F32, kind="ExternalInput").ap()
    vw = nc.dram_tensor("vw", [DH, D], F32, kind="ExternalInput").ap()
    ow = nc.dram_tensor("ow", [D, D], F32, kind="ExternalInput").ap()
    w1 = nc.dram_tensor("w1", [F, D], F32, kind="ExternalInput").ap()
    w2 = nc.dram_tensor("w2", [D, F], F32, kind="ExternalInput").ap()
    consts_h = nc.dram_tensor("consts_hbm", [128, 8], F32, kind="ExternalInput").ap()
    ident_h = nc.dram_tensor("ident_hbm", [128, 128], BF16, kind="ExternalInput").ap()
    identf_h = nc.dram_tensor("identf_hbm", [128, 128], F32, kind="ExternalInput").ap()
    onescol_h = nc.dram_tensor("onescol_hbm", [128, 1], F32, kind="ExternalInput").ap()
    onesrow_h = nc.dram_tensor("onesrow_hbm", [1, 128], F32, kind="ExternalInput").ap()
    out_h = nc.dram_tensor("out", [TOK, D], F32, kind="ExternalOutput").ap()
    sp_qwt = nc.dram_tensor("sp_qwt", [16 * 16 * 128 * 128], BF16)
    sp_owt = nc.dram_tensor("sp_owt", [D * D], BF16)
    sp_w1t = nc.dram_tensor("sp_w1t", [16 * 16 * 128 * 512], BF16)
    sp_w2t = nc.dram_tensor("sp_w2t", [4 * 64 * 128 * 512], BF16)
    sp_xit = nc.dram_tensor("sp_xit", [4 * 16 * 4 * 128 * 128], BF16)

    from contextlib import ExitStack
    es = ExitStack()

    BASE = 16512
    TOPB = 229344
    _cur = [BASE]
    def at(name, shape, dt, off=None):
        if off is None:
            off = _cur[0]
            nbytes = shape[1] * {F32: 4, BF16: 2, I32: 4}[dt]
            _cur[0] = (off + nbytes + 63) // 64 * 64
            assert _cur[0] <= TOPB, (name, _cur[0])
        return nc.alloc_sbuf_tensor_at(name, shape, dt, offset=off)

    consts = at("consts", [128, 8], F32)
    ident = at("ident", [128, 128], BF16)
    onescol = at("onescol", [128, 1], F32)
    onesrow = at("onesrow", [1, 128], F32)
    ln_s = at("ln_s", [128, 6], F32)
    s1col = at("s1col", [128, 2], F32)
    smax1 = at("smax1", [128, NOWN], F32)
    mrow = at("mrow", [1, 4], F32)
    mbc = at("mbc", [128, 4], F32)
    thr = at("thr", [128, 4], F32)
    c1c2 = at("c1c2", [128, 10], F32)
    hmax = at("hmax", [128, 6], F32)
    mpart = at("mpart", [128, 64], F32)
    identf = at("identf", [128, 128], F32)
    rcp = at("rcp", [1, QG], F32)
    P_END = _cur[0]

    QT = at("QT", [128, 16 * TOK], BF16, off=P_END)
    KT = at("KT", [128, S], BF16, off=P_END + 65536)
    Vn = at("Vn", [128, NKT * 128], BF16, off=P_END + 73728)
    E1 = P_END + 81920
    eT = at("eT", [128, 2 * NKT * QG], BF16, off=E1)
    attnT = at("attnT", [128, H * QG], BF16, off=E1 + 32768)
    E3 = E1 + 40960
    wqa = at("wqa", [128, 2048], BF16, off=E1)
    wqb = at("wqb", [128, 2048], BF16, off=E1 + 4096)
    wqn = at("wqn", [128, 2048], BF16, off=E1 + 8192)
    wcb = [at("wcb0", [128, D], F32, off=E1 + 12288),
           at("wcb1", [128, D], F32, off=E1 + 20480)]
    stageW = [at("stageW0", [128, 2048], BF16, off=E1 + 28672),
              at("stageW1", [128, 2048], BF16, off=E1 + 32768)]
    wcbD = [at(f"wcbD{i}", [128, D], F32, off=E3 + i * 8192) for i in range(5)]
    wqaD = at("wqaD", [128, 2048], BF16, off=E3 + 40960)
    wqbD = at("wqbD", [128, 2048], BF16, off=E3 + 45056)
    wqnD = at("wqnD", [128, 2048], BF16, off=E3 + 49152)
    stageWD = [at("stageWD0", [128, 2048], BF16, off=E3 + 53248),
               at("stageWD1", [128, 2048], BF16, off=E3 + 57344)]
    o = E3
    xbuf = [at("xbuf0", [128, D], F32, off=o), at("xbuf1", [128, D], F32, off=o + 8192)]
    xnf = [at("xnf0", [128, D], F32, off=o), at("xnf1", [128, D], F32, off=o + 8192)]
    o += 16384
    xnb = at("xnb", [128, D], BF16, off=o); o += 4096
    kvb = at("kvb", [128, D], BF16, off=o); o += 4096
    xnt = at("xnt", [128, 16 * 512], BF16, off=o); o += 16384
    kwt = at("kwt", [128, 16 * 128], BF16, off=o); o += 4096
    vwt = at("vwt", [128, 16 * 128], BF16, off=o); o += 4096
    HQW = 512
    qz = at("qz", [128, HQW], F32, off=o)
    qsa = at("qsa", [128, HQW], BF16, off=o + 2048)
    qsb = at("qsb", [128, HQW], BF16, off=o + 3072)
    qc = at("qc", [128, HQW], F32, off=o + 4096)
    qcand = at("qcand", [128, HQW], BF16, off=o + 6144)
    xsq = at("xsq", [128, D], F32, off=o)
    o += 8192
    qi = at("qi", [128, HQW], I32, off=o)
    qf2 = at("qf2", [128, HQW], F32, off=o + 4096)
    xn32 = at("xn32", [128, D], F32, off=o)
    o += 8192
    QRO = o
    qr = at("qr", [128, D], BF16, off=o); o += 4096
    stageX = at("stageX", [128, 2048], BF16, off=o); o += 4096
    stageV = at("stageV", [128, 512], BF16, off=o); o += 1024
    qwtb = [at("qwtb0", [128, 16 * 128], BF16, off=o), at("qwtb1", [128, 16 * 128], BF16, off=o + 4096)]
    o += 8192
    assert o <= TOPB, o
    owtb = [at("owtb0", [128, 512], BF16, off=QRO), at("owtb1", [128, 512], BF16, off=QRO + 1024)]
    xres = at("xres", [128, 2 * 512], F32, off=QRO + 2048)
    outp = [at("outp0", [128, 512], F32, off=QRO + 6144), at("outp1", [128, 512], F32, off=QRO + 8192)]
    rcpb = at("rcpb", [128, QG], F32, off=QRO + 10240)
    dsum = at("dsum", [128, QG], F32, off=QRO + 11264)

    h_sb = at("h_sb", [128, 4 * F], BF16, off=P_END)
    h16 = at("h16", [128, 4 * F], F16, off=P_END)
    hqT = at("hqT", [128, 64 * 512], BF16, off=P_END + 65536)
    f0 = P_END + 131072
    xitb = at("xitb", [128, 16 * 512], BF16, off=f0); f0 += 16384
    w1tb = [at("w1tb0", [128, 16 * 512], BF16, off=f0), at("w1tb1", [128, 16 * 512], BF16, off=f0 + 16384)]
    f0 += 32768
    w2tb = [at("w2tb0", [128, 4 * 512], BF16, off=f0), at("w2tb1", [128, 4 * 512], BF16, off=f0 + 4096)]
    f0 += 8192
    parb = [at(f"parb{i}", [128, 512], F32, off=f0 + i * 2048) for i in range(4)]
    f0 += 8192
    QW = 512
    fz = at("fz", [128, QW], F32, off=f0); f0 += 2048
    fi = at("fi", [128, QW], I32, off=f0); f0 += 2048
    fc_ = at("fc_", [128, QW], F32, off=f0); f0 += 2048
    ff1 = at("ff1", [128, QW], F32, off=f0); f0 += 2048
    ff2 = at("ff2", [128, QW], F32, off=f0)
    fsa = at("fsa", [128, QW], BF16, off=f0)
    fsb = at("fsb", [128, QW], BF16, off=f0 + 1024)
    f0 += 2048
    fr1 = at("fr1", [128, QW], F32, off=f0)
    fcand = at("fcand", [128, QW], BF16, off=f0)
    f0 += 2048
    assert f0 <= TOPB, f0

    PS0 = es.enter_context(nc.psum_tensor("PS0", [128, 1024], BF16))
    PS1 = es.enter_context(nc.psum_tensor("PS1", [128, 1024], BF16))
    PSf = [es.enter_context(nc.psum_tensor(f"PSf{i}", [128, 512], F32)) for i in range(6)]
    PS2, PS3, PS4, PS5, PS6, PS7 = PSf

    semnames = ("dX", "dW", "dSP", "dRD", "dRes", "dPar", "dOut", "T", "V", "Sx", "hz")
    sems = {n: Sem(es.enter_context(nc.semaphore(n))) for n in semnames}
    dX, dW, dSP, dRD, dRes, dPar, dOut, T, V, Sx, hz = (sems[n] for n in semnames)

    P = Builder()
    ap = bass.AP
    bu = {}

    def bwait(eng, key):
        if key in bu:
            sem, cnt = bu[key]
            P.wait(eng, sem, cnt)

    def cst(i):
        return consts[:, i:i + 1]

    def vh(fn):
        P.op("vector", fn, inc=hz)
        P.wait("vector", hz, hz.n)

    P.op("scalar", lambda e: e.dma_start(consts[:], consts_h[:]), inc=dX, k=16)
    P.op("scalar", lambda e: e.dma_start(ident[:], ident_h[:]), inc=dX, k=16)
    P.op("scalar", lambda e: e.dma_start(identf[:], identf_h[:]), inc=dX, k=16)
    P.op("scalar", lambda e: e.dma_start(onescol[:], onescol_h[:]), inc=dX, k=16)
    P.op("scalar", lambda e: e.dma_start(onesrow[:], onesrow_h[:]), inc=dX, k=16)
    P.op("scalar", lambda e: e.dma_start(xbuf[0][:], kw[:, :]), inc=dX, k=16)
    P.op("scalar", lambda e: e.dma_start(xbuf[1][:], vw[:, :]), inc=dX, k=16)
    P.wait("scalar", dX, dX.n)
    P.wait("vector", dX, dX.n)
    P.op("vector", lambda e: e.tensor_copy(xnb[:], xbuf[0][:]), inc=V)
    P.op("vector", lambda e: e.tensor_copy(kvb[:], xbuf[1][:]), inc=V)
    KV_CAST_V = V.n

    tr_user = {0: 0, 1: 0}

    def emit_tile_transposes(src_sb, src_off_fn, dst_sb, dst_off_fn, ntiles,
                             gate_V=None, evac_dtype_note=None):
        for g in range((ntiles + 7) // 8):
            nb = min(8, ntiles - g * 8)
            pb = PS0 if g % 2 == 0 else PS1
            if gate_V:
                P.wait("tensor", V, gate_V)
            P.wait("tensor", V, tr_user[g % 2])
            def tfn(e, g=g, nb=nb, pb=pb):
                for j in range(nb - 1):
                    e.transpose(pb[:, j * 128:(j + 1) * 128],
                                src_sb[:, src_off_fn(g * 8 + j):src_off_fn(g * 8 + j) + 128],
                                ident[:])
                return e.transpose(pb[:, (nb - 1) * 128:nb * 128],
                                   src_sb[:, src_off_fn(g * 8 + nb - 1):src_off_fn(g * 8 + nb - 1) + 128],
                                   ident[:])
            P.op("tensor", tfn, inc=T)
            P.wait("vector", T, T.n)
            def vfn(e, g=g, nb=nb, pb=pb):
                for j in range(nb - 1):
                    e.tensor_copy(dst_sb[:, dst_off_fn(g * 8 + j):dst_off_fn(g * 8 + j) + 128],
                                  pb[:, j * 128:(j + 1) * 128])
                return e.tensor_copy(dst_sb[:, dst_off_fn(g * 8 + nb - 1):dst_off_fn(g * 8 + nb - 1) + 128],
                                     pb[:, (nb - 1) * 128:nb * 128])
            P.op("vector", vfn, inc=V)
            tr_user[g % 2] = V.n
        return V.n

    emit_tile_transposes(xnb, lambda t: t * 128, kwt, lambda t: t * 128, 16,
                         gate_V=KV_CAST_V)
    emit_tile_transposes(kvb, lambda t: t * 128, vwt, lambda t: t * 128, 16,
                         gate_V=KV_CAST_V)
    KVW_V = V.n

    cast_done = {}
    spill_done = {}

    def emit_co_spill(src_ap, kind):
        for ch in range(16):
            b = wcb[ch % 2]
            bwait("sync", ("wcb", ch % 2))
            P.op("sync", lambda e, ch=ch, b=b, src=src_ap:
                 e.dma_start(b[:], src[ch * 128:(ch + 1) * 128, :]), inc=dW, k=16)
            P.wait("vector", dW, dW.n)
            st = stageW[ch % 2]
            bwait("vector", ("stageW", ch % 2))
            P.op("vector", lambda e, b=b, st=st: e.tensor_copy(st[:], b[:]), inc=V)
            bu[("wcb", ch % 2)] = (V, V.n)
            cv = V.n
            emit_tile_transposes(st, lambda t: t * 128, wqn, lambda t: t * 128, 16,
                                 gate_V=cv)

            P.wait("scalar", V, V.n)
            if kind == "q":
                dst = ap(sp_qwt, ch * 16 * 128 * 128,
                         [[128, 128], [128 * 128, 16], [1, 128]])
            else:
                dst = ap(sp_owt, ch * 128,
                         [[D, 128], [128 * D, 16], [1, 128]])
            src_st = ap(wqn, 0, [[2048, 128], [128, 16], [1, 128]])
            P.op("scalar", lambda e, dst=dst, src_st=src_st:
                 e.dma_start(dst, src_st), inc=dSP, k=16)
            bu[("stageW", ch % 2)] = (dSP, dSP.n)
            bu["wqn"] = (dSP, dSP.n)
            P.wait("vector", dSP, dSP.n)

    emit_co_spill(qw, "q")
    QWT_SPILLED = dSP.n
    emit_co_spill(ow, "o")
    OWT_SPILLED = dSP.n


    def emit_mpass_chunk(src_ap, ch, w2mode, bufs):
        b = bufs[ch % 5]
        bwait("gpsimd", ("wcbD", ch % 5))
        if w2mode:
            r, fs = ch // 4, ch % 4
            srcap = src_ap[r * 128:(r + 1) * 128, fs * 2048:(fs + 1) * 2048]
        else:
            srcap = src_ap[ch * 128:(ch + 1) * 128, :]
        P.op("gpsimd", lambda e, b=b, srcap=srcap: e.dma_start(b[:], srcap), inc=dW, k=16)
        P.wait("vector", dW, dW.n)
        P.op("vector", lambda e, b=b, ch=ch:
             e.tensor_reduce(mpart[:, ch:ch + 1], b[:], AX.X, ALU.add,
                             apply_absolute_value=True), inc=V)
        bu[("wcbD", ch % 5)] = (V, V.n)

    def emit_mfinal(acc_col):
        P.wait("vector", V, V.n)
        n = 64
        while n > 1:
            h = n // 2
            vh(lambda e, h=h: e.tensor_add(mpart[:, 0:h], mpart[:, 0:h], mpart[:, h:2 * h]))
            n = h
        P.wait("tensor", hz, hz.n)
        if psgate[0]:
            P.wait("tensor", hz, psgate[0])
        P.op("tensor", lambda e: e.transpose(PS7[0:1, 0:128], mpart[:, 0:1], identf[:]), inc=T)
        P.wait("vector", T, T.n)
        vh(lambda e: e.tensor_copy(mpart[0:1, 0:64], PS7[0:1, 0:64]))
        vh(lambda e: e.tensor_add(mpart[0:1, 0:64], mpart[0:1, 0:64], PS7[0:1, 64:128]))
        psgate[0] = hz.n
        n = 64
        while n > 1:
            h = n // 2
            vh(lambda e, h=h: e.tensor_add(mpart[0:1, 0:h], mpart[0:1, 0:h], mpart[0:1, h:2 * h]))
            n = h
        P.op("vector", lambda e, acc_col=acc_col:
             e.tensor_scalar(out=mrow[0:1, acc_col:acc_col + 1], in0=mpart[0:1, 0:1],
                             scalar1=1.0 / (F * D), scalar2=1e-5,
                             op0=ALU.mult, op1=ALU.max), inc=V)
        P.wait("vector", V, V.n)

    def emit_thr():
        P.wait("tensor", V, V.n)
        if psgate[0]:
            P.wait("tensor", hz, psgate[0])
        P.op("tensor", lambda e: e.matmul(PS7[:, 4:6], onesrow[:], mrow[0:1, 0:2],
                                          start=True, stop=True), inc=T)
        P.wait("vector", T, T.n)
        vh(lambda e: e.tensor_copy(mbc[:, 0:2], PS7[:, 4:6]))
        psgate[0] = hz.n
        vh(lambda e: e.tensor_scalar(out=thr[:, 0:1], in0=mbc[:, 0:1], scalar1=0.5,
                                     scalar2=None, op0=ALU.mult))
        vh(lambda e: e.tensor_scalar(out=thr[:, 1:2], in0=mbc[:, 0:1], scalar1=-0.5,
                                     scalar2=None, op0=ALU.mult))
        vh(lambda e: e.tensor_scalar(out=thr[:, 2:3], in0=mbc[:, 1:2], scalar1=0.5,
                                     scalar2=None, op0=ALU.mult))
        P.op("vector", lambda e: e.tensor_scalar(out=thr[:, 3:4], in0=mbc[:, 1:2],
                                                 scalar1=-0.5, scalar2=None,
                                                 op0=ALU.mult), inc=V)
        thrv[0] = V.n

    def emit_wquant_chunk(src_ap, ch, thr_i, which):
        b = wcbD[ch % 5]
        bwait("gpsimd", ("wcbD", ch % 5))
        if which == "w2":
            r, fs = ch // 4, ch % 4
            srcap = src_ap[r * 128:(r + 1) * 128, fs * 2048:(fs + 1) * 2048]
        else:
            srcap = src_ap[ch * 128:(ch + 1) * 128, :]
        P.op("gpsimd", lambda e, b=b, srcap=srcap: e.dma_start(b[:], srcap), inc=dW, k=16)
        P.wait("scalar", dW, dW.n)
        P.wait("scalar", V, thrv[0])
        P.op("scalar", lambda e, b=b, thr_i=thr_i:
             e.sign(wqaD[:], b[:], bias=thr[:, thr_i + 1:thr_i + 2]), inc=Sx)
        P.op("scalar", lambda e, b=b, thr_i=thr_i:
             e.sign(wqbD[:], b[:], bias=thr[:, thr_i:thr_i + 1]), inc=Sx)
        bu[("wcbD", ch % 5)] = (Sx, Sx.n)
        P.wait("vector", Sx, Sx.n)
        bwait("vector", "wqnD")
        P.op("vector", lambda e: e.tensor_add(wqnD[:], wqaD[:], wqbD[:]), inc=V)
        P.wait("vector", V, V.n)
        st = stageWD[ch % 2]
        bwait("vector", ("stageWD", ch % 2))
        emit_tile_transposes(wqnD, lambda t: t * 128, st, lambda t: t * 128, 16,
                             gate_V=V.n)
        bu["wqnD"] = (T, T.n)
        P.wait("scalar", V, V.n)
        if which == "w1":
            f0 = ch
            base = (f0 // 4) * (128 * 512) + (f0 % 4) * 128
            dst = ap(sp_w1t, base, [[512, 128], [16 * 128 * 512, 16], [1, 128]])
        else:
            r, fs = ch // 4, ch % 4
            base = (r // 4) * (64 * 128 * 512) + fs * 16 * (128 * 512) + (r % 4) * 128
            dst = ap(sp_w2t, base, [[512, 128], [128 * 512, 16], [1, 128]])
        src_st = ap(st, 0, [[2048, 128], [128, 16], [1, 128]])
        P.op("scalar", lambda e, dst=dst, src_st=src_st:
             e.dma_start(dst, src_st), inc=dSP, k=16)
        bu[("stageWD", ch % 2)] = (dSP, dSP.n)

    psgate = [0]
    thrv = [0]
    wq_i = [0]
    wtasks = []
    for ch in range(64):
        wtasks.append(lambda ch=ch: emit_mpass_chunk(w1, ch, False, wcbD))
    wtasks.append(lambda: emit_mfinal(0))
    for ch in range(64):
        wtasks.append(lambda ch=ch: emit_mpass_chunk(w2, ch, True, wcbD))
    wtasks.append(lambda: emit_mfinal(1))
    wtasks.append(lambda: emit_thr())
    for ch in range(64):
        wtasks.append(lambda ch=ch: emit_wquant_chunk(w1, ch, 0, "w1"))
    for ch in range(64):
        wtasks.append(lambda ch=ch: emit_wquant_chunk(w2, ch, 2, "w2"))

    xn_done = {}
    for tg in range(8):
        for j in range(4):
            c = tg * 4 + j
            own = c < NOWN
            buf = xbuf[c % 2]
            if c >= 2:
                P.wait("scalar", V, xn_done[c - 2])
            else:
                P.wait("scalar", V, KVW_V)
            P.op("scalar", lambda e, c=c, buf=buf:
                 e.dma_start(buf[:], xb[c * 128:(c + 1) * 128, :]), inc=dX, k=16)
            P.wait("vector", dX, dX.n)
            vh(lambda e, buf=buf: e.tensor_reduce(ln_s[:, 0:1], buf[:], AX.X, ALU.add))
            vh(lambda e, buf=buf:
               e.scalar_tensor_tensor(out=xsq[:], in0=buf[:], scalar=1.0, in1=buf[:],
                                      op0=ALU.mult, op1=ALU.mult,
                                      accum_out=ln_s[:, 1:2]))
            vh(lambda e: e.tensor_scalar(out=ln_s[:, 2:3], in0=ln_s[:, 0:1],
                                         scalar1=1.0 / D, scalar2=None, op0=ALU.mult))
            vh(lambda e: e.tensor_scalar(out=ln_s[:, 3:4], in0=ln_s[:, 1:2],
                                         scalar1=1.0 / D, scalar2=None, op0=ALU.mult))
            vh(lambda e: e.tensor_mul(ln_s[:, 4:5], ln_s[:, 2:3], ln_s[:, 2:3]))
            vh(lambda e: e.tensor_sub(ln_s[:, 4:5], ln_s[:, 3:4], ln_s[:, 4:5]))
            vh(lambda e: e.tensor_scalar(out=ln_s[:, 4:5], in0=ln_s[:, 4:5],
                                         scalar1=EPS, scalar2=None, op0=ALU.add))
            P.op("vector", lambda e: e.reciprocal(ln_s[:, 5:6], ln_s[:, 4:5]), inc=V)
            P.wait("scalar", V, V.n)
            P.op("scalar", lambda e: e.activation(ln_s[:, 5:6], ln_s[:, 5:6], ACT.Sqrt),
                 inc=Sx)
            vh(lambda e: e.tensor_scalar(out=ln_s[:, 3:4], in0=ln_s[:, 2:3],
                                         scalar1=-1.0, scalar2=None, op0=ALU.mult))
            P.op("vector", lambda e: e.engine_nop(), inc=V)
            P.wait("scalar", V, V.n)
            xf = xnf[c % 2]
            P.op("scalar", lambda e, buf=buf: e.add(xn32[:], buf[:], ln_s[:, 3:4]), inc=Sx)
            P.op("scalar", lambda e, xf=xf: e.mul(xf[:], xn32[:], ln_s[:, 5:6]), inc=Sx)
            bwait("scalar", "xnb")
            P.op("scalar", lambda e, xf=xf: e.copy(xnb[:], xf[:]), inc=Sx)
            P.wait("vector", Sx, Sx.n)
            P.op("vector", lambda e: e.engine_nop(), inc=V)
            xn_V = V.n

            if own:
                oi = c
                vh(lambda e, oi=oi, xf=xf: e.tensor_reduce(smax1[:, oi:oi + 1], xf[:], AX.X,
                                                    ALU.max, apply_absolute_value=True))
                vh(lambda e, oi=oi: e.tensor_scalar(out=smax1[:, oi:oi + 1],
                                                    in0=smax1[:, oi:oi + 1], scalar1=1e-5,
                                                    scalar2=None, op0=ALU.max))
                vh(lambda e, oi=oi: e.reciprocal(s1col[:, 0:1], smax1[:, oi:oi + 1]))
                vh(lambda e: e.tensor_scalar(out=s1col[:, 0:1], in0=s1col[:, 0:1],
                                             scalar1=127.0, scalar2=None, op0=ALU.mult))
                bwait("vector", "qr")
                P.op("vector", lambda e: e.engine_nop(), inc=V)
                P.wait("scalar", V, V.n)
                for hh in range(D // HQW):
                    s0 = hh * HQW
                    P.op("scalar", lambda e, s0=s0, xf=xf:
                         e.mul(qz[:], xf[:, s0:s0 + HQW], s1col[:, 0:1]), inc=Sx)
                    P.op("scalar", lambda e: e.copy(qi[:], qz[:]), inc=Sx)
                    P.op("scalar", lambda e: e.copy(qc[:], qi[:]), inc=Sx)
                    P.wait("vector", Sx, Sx.n)
                    P.op("vector", lambda e: e.tensor_sub(qf2[:], qz[:], qc[:]), inc=V)
                    P.wait("scalar", V, V.n)
                    P.op("scalar", lambda e: e.sign(qsa[:], qf2[:], bias=cst(C_NHALF)), inc=Sx)
                    P.op("scalar", lambda e: e.sign(qsb[:], qf2[:], bias=cst(C_HALF)), inc=Sx)
                    P.wait("vector", Sx, Sx.n)
                    P.op("vector", lambda e: e.tensor_add(qcand[:], qc[:], qc[:]))
                    P.op("vector", lambda e: e.tensor_add(qsa[:], qsa[:], qsb[:]))
                    P.op("vector", lambda e, s0=s0:
                         e.tensor_add(qr[:, s0:s0 + HQW], qcand[:], qsa[:]), inc=V)
                xi_V = V.n
                P.wait("vector", V, xi_V)

            emit_tile_transposes(xnb, lambda t: t * 128,
                                 xnt, lambda t, j=j: t * 512 + j * 128, 16,
                                 gate_V=xn_V)
            bu["xnb"] = (T, T.n)
            if own:
                emit_tile_transposes(qr, lambda t: t * 128,
                                     stageX, lambda t: t * 128, 16, gate_V=xi_V)
                bu["qr"] = (T, T.n)
                qtr, cc = c // 4, c % 4
                base = qtr * (16 * 4 * 128 * 128) + cc * 128
                dst = ap(sp_xit, base, [[4 * 128, 128], [4 * 128 * 128, 16], [1, 128]])
                src_st = ap(stageX, 0, [[2048, 128], [128, 16], [1, 128]])
                P.wait("scalar", V, V.n)
                P.op("scalar", lambda e, dst=dst, src_st=src_st:
                     e.dma_start(dst, src_st), inc=dSP, k=16)
                P.wait("vector", dSP, dSP.n)
            xn_done[c] = V.n

        P.wait("tensor", V, V.n)
        def t_k(e, tg=tg):
            for dt in range(15):
                e.matmul(PS4[:, 0:512], kwt[:, dt * 128:(dt + 1) * 128],
                         xnt[:, dt * 512:(dt + 1) * 512], start=(dt == 0), stop=False)
            return e.matmul(PS4[:, 0:512], kwt[:, 15 * 128:16 * 128],
                            xnt[:, 15 * 512:16 * 512], start=False, stop=True)
        P.op("tensor", t_k, inc=T)
        def t_v(e, tg=tg):
            for dt in range(15):
                e.matmul(PS5[:, 0:512], vwt[:, dt * 128:(dt + 1) * 128],
                         xnt[:, dt * 512:(dt + 1) * 512], start=(dt == 0), stop=False)
            return e.matmul(PS5[:, 0:512], vwt[:, 15 * 128:16 * 128],
                            xnt[:, 15 * 512:16 * 512], start=False, stop=True)
        P.op("tensor", t_v, inc=T)
        P.wait("vector", T, T.n)
        P.op("vector", lambda e, tg=tg:
             e.tensor_copy(KT[:, tg * 512:(tg + 1) * 512], PS4[:, 0:512]), inc=V)
        P.op("vector", lambda e: e.tensor_copy(stageV[:, 0:512], PS5[:, 0:512]), inc=V)
        vt_V = V.n
        emit_tile_transposes(stageV, lambda t: t * 128,
                             Vn, lambda t, tg=tg: (tg * 4 + t) * 128, 4, gate_V=vt_V)
        P.wait("tensor", V, V.n)

        if tg < 4:
            for do in range(16):
                qb = qwtb[do % 2]
                bwait("sync", ("qwtb", do % 2))
                P.wait("sync", dSP, QWT_SPILLED)
                src = ap(sp_qwt, do * 16 * 128 * 128, [[128, 128], [128 * 128, 16], [1, 128]])
                P.op("sync", lambda e, qb=qb, src=src: e.dma_start(qb[:], src),
                     inc=dRD, k=16)
                P.wait("tensor", dRD, dRD.n)
                pbank = PS2 if do % 2 == 0 else PS3
                def t_q(e, qb=qb, pbank=pbank):
                    for dt in range(15):
                        e.matmul(pbank[:, 0:512], qb[:, dt * 128:(dt + 1) * 128],
                                 xnt[:, dt * 512:(dt + 1) * 512], start=(dt == 0),
                                 stop=False)
                    return e.matmul(pbank[:, 0:512], qb[:, 15 * 128:16 * 128],
                                    xnt[:, 15 * 512:16 * 512], start=False, stop=True)
                P.op("tensor", t_q, inc=T)
                bu[("qwtb", do % 2)] = (T, T.n)
                P.wait("vector", T, T.n)
                P.op("vector", lambda e, tg=tg, do=do, pbank=pbank:
                     e.tensor_copy(QT[:, do * TOK + tg * 512: do * TOK + (tg + 1) * 512],
                                   pbank[:, 0:512]), inc=V)
                P.wait("tensor", V, V.n)

    PROLOGUE_V = V.n

    P.wait("tensor", V, PROLOGUE_V)
    P.wait("scalar", V, PROLOGUE_V)
    sbank_user = {}
    pv_user = {}
    et_user = {}
    oproj_T = {}
    oproj_done_V = 0
    oev_prev = 0
    for qg in range(NQG):

        def emit_ktloop(h, qg=qg):
            hp = h % 2
            eoff = hp * NKT * QG
            pvbank = PS4 if hp == 0 else PS5
            if hp in pv_user:
                P.wait("tensor", V, pv_user[hp])
            if hp in et_user:
                P.wait("scalar", V, et_user[hp])
            def sc(kt):
                sbank = (PS2, PS3, PS6)[kt % 3]
                if kt % 3 in sbank_user:
                    P.wait("tensor", Sx, sbank_user[kt % 3])
                def t_sc(e, h=h, qg=qg, kt=kt, sbank=sbank):
                    return e.matmul(sbank[:, 0:QG], KT[:, kt * 128:(kt + 1) * 128],
                                    QT[:, h * TOK + qg * QG: h * TOK + (qg + 1) * QG],
                                    start=True, stop=True)
                P.op("tensor", t_sc, inc=T)
                P.wait("scalar", T, T.n)
                def s_exp(e, kt=kt, sbank=sbank, eoff=eoff):
                    return e.activation(eT[:, eoff + kt * QG: eoff + (kt + 1) * QG],
                                        sbank[:, 0:QG], ACT.Exp,
                                        bias=cst(C_SHIFT), scale=cst(C_SCALE))
                P.op("scalar", s_exp, inc=Sx)
                sbank_user[kt % 3] = Sx.n
            def pv(kt, expc):
                P.wait("tensor", Sx, expc)
                def t_pv(e, kt=kt, pvbank=pvbank, eoff=eoff):
                    return e.matmul(pvbank[:, 0:QG], Vn[:, kt * 128:(kt + 1) * 128],
                                    eT[:, eoff + kt * QG: eoff + (kt + 1) * QG],
                                    start=(kt == 0), stop=(kt == NKT - 1))
                P.op("tensor", t_pv, inc=(T if kt == NKT - 1 else None))
            expc = {}
            sc(0)
            expc[0] = Sx.n
            sc(1)
            expc[1] = Sx.n
            for kt in range(NKT):
                if kt + 2 < NKT:
                    sc(kt + 2)
                    expc[kt + 2] = Sx.n
                pv(kt, expc[kt])
            den_state[h] = (T.n, Sx.n, eoff, pvbank)

        def emit_denom(h):
            pv_T, exp_S, eoff, pvbank = den_state[h]
            hp = h % 2
            P.wait("vector", Sx, exp_S)
            def v_dn(e, eoff=eoff):
                src = ap(eT, eoff, [[2 * NKT * QG, 128], [1, QG], [QG, NKT]])
                return e.tensor_reduce(dsum[:], src, AX.X, ALU.add)
            P.op("vector", v_dn, inc=V)
            et_user[hp] = V.n
            P.wait("tensor", V, V.n)
            if den_ps6[0]:
                P.wait("tensor", hz, den_ps6[0])
            P.op("tensor", lambda e: e.matmul(PS7[0:1, 0:QG], onescol[:], dsum[:],
                                              start=True, stop=True), inc=T)
            P.wait("vector", T, T.n)
            P.op("vector", lambda e: e.reciprocal(rcp[0:1, :], PS7[0:1, 0:QG]), inc=V)
            P.wait("tensor", V, V.n)
            P.op("tensor", lambda e: e.matmul(PS7[:, 256:256 + QG], onesrow[:],
                                              rcp[0:1, :], start=True, stop=True), inc=T)
            P.wait("vector", T, max(T.n, pv_T))
            vh(lambda e: e.tensor_copy(rcpb[:], PS7[:, 256:256 + QG]))
            den_ps6[0] = hz.n
            def v_nrm(e, h=h, pvbank=pvbank):
                return e.tensor_mul(attnT[:, h * QG:(h + 1) * QG], pvbank[:, 0:QG],
                                    rcpb[:])
            P.op("vector", v_nrm, inc=V)
            pv_user[hp] = V.n

        den_state = {}
        den_ps6 = psgate
        for h in range(H):
            emit_ktloop(h)
            if h > 0:
                emit_denom(h - 1)
        emit_denom(H - 1)
        heads_V = V.n
        for dc in range(4):
            if oproj_done_V or dc > 0:
                P.wait("scalar", V, oev_prev)
            for tt in range(2):
                P.op("scalar", lambda e, qg=qg, tt=tt, dc=dc:
                     e.dma_start(xres[:, tt * 512:(tt + 1) * 512],
                                 xb[qg * 256 + tt * 128: qg * 256 + (tt + 1) * 128,
                                    dc * 512:(dc + 1) * 512]),
                     inc=dRes, k=16)
            xres_d = dRes.n
            for hd in range(16):
                obuf = owtb[hd % 2]
                bwait("sync", ("owtb", hd % 2))
                P.wait("sync", dSP, OWT_SPILLED)
                src = ap(sp_owt, hd * 128 * D + dc * 512, [[D, 128], [1, 512]])
                P.op("sync", lambda e, obuf=obuf, src=src: e.dma_start(obuf[:], src),
                     inc=dRD, k=16)
                P.wait("tensor", dRD, dRD.n)
                if hd == 0:
                    P.wait("tensor", V, heads_V)
                def t_op(e, hd=hd, obuf=obuf):
                    e.matmul(PS4[:, 0:512],
                             attnT[:, hd * QG: hd * QG + 128], obuf[:],
                             start=(hd == 0), stop=(hd == 15))
                    return e.matmul(PS5[:, 0:512],
                                    attnT[:, hd * QG + 128: hd * QG + 256], obuf[:],
                                    start=(hd == 0), stop=(hd == 15))
                P.op("tensor", t_op, inc=T)
                bu[("owtb", hd % 2)] = (T, T.n)
            P.wait("vector", T, T.n)
            P.wait("vector", dRes, xres_d)
            if dc > 0:
                P.wait("vector", dPar, dPar.n)
            for tt in range(2):
                P.op("vector", lambda e, tt=tt:
                     e.tensor_add(outp[tt][:], (PS4 if tt == 0 else PS5)[:, 0:512],
                                  xres[:, tt * 512:(tt + 1) * 512]),
                     inc=V)
            oev_V = V.n
            oev_prev = oev_V
            P.wait("tensor", V, oev_V)
            P.wait("scalar", V, oev_V)
            for tt in range(2):
                P.op("scalar", lambda e, qg=qg, tt=tt, dc=dc:
                     e.dma_start(out_h[qg * 256 + tt * 128: qg * 256 + (tt + 1) * 128,
                                       dc * 512:(dc + 1) * 512], outp[tt][:]),
                     inc=dPar, k=16)
        oproj_done_V = V.n
        if wq_i[0] == 0:
            P.wait("gpsimd", V, heads_V)
        for _ in range(33):
            if wq_i[0] < len(wtasks):
                wtasks[wq_i[0]]()
                wq_i[0] += 1

    while wq_i[0] < len(wtasks):
        wtasks[wq_i[0]]()
        wq_i[0] += 1
    W1_SPILLED = dSP.n
    W2_SPILLED = dSP.n

    ATTN_PAR = dPar.n

    ffn1_T = {}
    ffn1_S = {}
    ffn2_T = {}
    ffn2_ev_V = 0
    xit_T_prev = 0
    for q in range(4):
        P.wait("sync", dSP, dSP.n)
        P.wait("sync", T, xit_T_prev)
        src = ap(sp_xit, q * (16 * 4 * 128 * 128), [[4 * 128, 128], [4 * 128 * 128, 16], [1, 4 * 128]])
        P.op("sync", lambda e, src=src: e.dma_start(xitb[:], src), inc=dRD, k=16)
        xit_rd = dRD.n
        P.wait("vector", V, PROLOGUE_V)
        for tt in range(4):
            P.op("vector", lambda e, q=q, tt=tt:
                 e.tensor_scalar(out=c1c2[:, tt:tt + 1],
                                 in0=smax1[:, q * 4 + tt:q * 4 + tt + 1],
                                 scalar1=mbc[:, 0:1], scalar2=1.0 / 127.0 / 4.0,
                                 op0=ALU.mult, op1=ALU.mult),
                 inc=(V if tt == 3 else None))
        c1_V = V.n

        for fcx in range(16):
            wb = w1tb[fcx % 2]
            bwait("sync", ("w1tb", fcx % 2))
            P.wait("sync", dSP, W1_SPILLED)
            src = ap(sp_w1t, fcx * 128 * 512, [[512, 128], [16 * 128 * 512, 16], [1, 512]])
            P.op("sync", lambda e, wb=wb, src=src: e.dma_start(wb[:], src), inc=dRD, k=16)
            P.wait("tensor", dRD, max(dRD.n, xit_rd))
            for tt in range(4):
                pbank = PS2 if (fcx * 4 + tt) % 2 == 0 else PS3
                key = (q, fcx * 4 + tt - 2)
                if key in ffn1_S:
                    P.wait("tensor", Sx, ffn1_S[key])
                def t_f1(e, tt=tt, wb=wb, pbank=pbank):
                    for dt in range(15):
                        e.matmul(pbank[:, 0:512],
                                 xitb[:, dt * 512 + tt * 128: dt * 512 + tt * 128 + 128],
                                 wb[:, dt * 512:(dt + 1) * 512],
                                 start=(dt == 0), stop=False)
                    return e.matmul(pbank[:, 0:512],
                                    xitb[:, 15 * 512 + tt * 128: 15 * 512 + tt * 128 + 128],
                                    wb[:, 15 * 512:16 * 512], start=False, stop=True)
                P.op("tensor", t_f1, inc=T)
                ffn1_T[(q, fcx)] = T.n
                bu[("w1tb", fcx % 2)] = (T, T.n)
                P.wait("scalar", T, T.n)
                P.wait("scalar", V, c1_V)
                def s_silu(e, tt=tt, fcx=fcx, pbank=pbank):
                    return e.activation(h16[:, tt * F + fcx * 512: tt * F + fcx * 512 + 512],
                                        pbank[:, 0:512], ACT.Silu, scale=c1c2[:, tt:tt + 1])
                P.op("scalar", s_silu, inc=Sx)
                ffn1_S[(q, fcx * 4 + tt)] = Sx.n
        ffn1_done_S = Sx.n

        P.wait("vector", Sx, ffn1_done_S)
        for tt in range(4):
            vh(lambda e, tt=tt: e.tensor_reduce(hmax[:, 0:1], h16[:, tt * F:(tt + 1) * F],
                                                AX.X, ALU.max, apply_absolute_value=True))
            vh(lambda e: e.tensor_scalar(out=hmax[:, 1:2], in0=hmax[:, 0:1], scalar1=1e-5,
                                         scalar2=None, op0=ALU.max))
            vh(lambda e: e.reciprocal(hmax[:, 2:3], hmax[:, 1:2]))
            vh(lambda e: e.tensor_scalar(out=hmax[:, 2:3], in0=hmax[:, 2:3], scalar1=127.0,
                                         scalar2=None, op0=ALU.mult))
            vh(lambda e, tt=tt: e.tensor_scalar(out=c1c2[:, 4 + tt:5 + tt], in0=hmax[:, 1:2],
                                                scalar1=mbc[:, 1:2], scalar2=1.0 / 127.0 / 4.0,
                                                op0=ALU.mult, op1=ALU.mult))
            P.op("vector", lambda e: e.engine_nop(), inc=V)
            P.wait("scalar", V, V.n)
            for cq in range(F // QW):
                sl0 = tt * F + cq * QW
                P.op("scalar", lambda e, sl0=sl0:
                     e.mul(fz[:], h16[:, sl0:sl0 + QW], hmax[:, 2:3]), inc=Sx)
                P.op("scalar", lambda e: e.copy(fi[:], fz[:]), inc=Sx)
                P.op("scalar", lambda e: e.copy(fc_[:], fi[:]), inc=Sx)
                P.wait("vector", Sx, Sx.n)
                P.op("vector", lambda e: e.tensor_sub(ff1[:], fz[:], fc_[:]), inc=V)
                P.wait("scalar", V, V.n)
                P.op("scalar", lambda e: e.sign(fsa[:], ff1[:], bias=cst(C_NHALF)), inc=Sx)
                P.op("scalar", lambda e: e.sign(fsb[:], ff1[:], bias=cst(C_HALF)), inc=Sx)
                P.wait("vector", Sx, Sx.n)
                P.op("vector", lambda e: e.tensor_add(fcand[:], fc_[:], fc_[:]))
                P.op("vector", lambda e: e.tensor_add(fsa[:], fsa[:], fsb[:]))
                P.op("vector", lambda e, sl0=sl0:
                     e.tensor_add(h_sb[:, sl0:sl0 + QW], fcand[:], fsa[:]), inc=V)
            P.op("vector", lambda e: e.engine_nop(), inc=V)
            hq_V = V.n
            emit_tile_transposes(h_sb, lambda t, tt=tt: tt * F + t * 128,
                                 hqT, lambda t, tt=tt: t * 512 + tt * 128, 64,
                                 gate_V=hq_V)
        hqT_V = V.n

        for dq in range(4):
            P.wait("scalar", dPar, ATTN_PAR)
            if ffn2_ev_V:
                P.wait("scalar", V, ffn2_ev_V)
            for tt in range(4):
                P.op("scalar", lambda e, q=q, tt=tt, dq=dq:
                     e.dma_start(parb[tt][:],
                                 out_h[q * 512 + tt * 128: q * 512 + (tt + 1) * 128,
                                       dq * 512:(dq + 1) * 512]), inc=dRes, k=16)
            par_d = dRes.n
            for grp in range(16):
                wb2 = w2tb[grp % 2]
                bwait("sync", ("w2tb", grp % 2))
                P.wait("sync", dSP, W2_SPILLED)
                src = ap(sp_w2t, dq * (64 * 128 * 512) + grp * 4 * (128 * 512),
                         [[512, 128], [128 * 512, 4], [1, 512]])
                P.op("sync", lambda e, wb2=wb2, src=src: e.dma_start(wb2[:], src),
                     inc=dRD, k=16)
                P.wait("tensor", dRD, dRD.n)
                if grp == 0:
                    P.wait("tensor", V, hqT_V)
                    if ffn2_ev_V:
                        P.wait("tensor", V, ffn2_ev_V)
                def t_f2(e, grp=grp, wb2=wb2):
                    last = None
                    for j in range(4):
                        ft = grp * 4 + j
                        for tt in range(4):
                            last = e.matmul(PSf[2 + tt][:, 0:512],
                                            hqT[:, ft * 512 + tt * 128: ft * 512 + tt * 128 + 128],
                                            wb2[:, j * 512:(j + 1) * 512],
                                            start=(ft == 0), stop=(ft == 63))
                    return last
                P.op("tensor", t_f2, inc=T)
                ffn2_T[(q, dq, grp)] = T.n
                bu[("w2tb", grp % 2)] = (T, T.n)
            P.wait("vector", T, T.n)
            P.wait("vector", dRes, par_d)
            for tt in range(4):
                P.op("vector", lambda e, tt=tt:
                     e.scalar_tensor_tensor(out=parb[tt][:], in0=PSf[2 + tt][:, 0:512],
                                            scalar=c1c2[:, 4 + tt:5 + tt],
                                            in1=parb[tt][:],
                                            op0=ALU.mult, op1=ALU.add),
                     inc=(V if tt == 3 else None))
            ffn2_ev_V = V.n
            P.wait("scalar", V, ffn2_ev_V)
            for tt in range(4):
                P.op("scalar", lambda e, q=q, tt=tt, dq=dq:
                     e.dma_start(out_h[q * 512 + tt * 128: q * 512 + (tt + 1) * 128,
                                       dq * 512:(dq + 1) * 512], parb[tt][:]),
                     inc=dOut, k=16)
        xit_T_prev = ffn1_T[(q, 15)]

    P.wait("sync", dOut, dOut.n)
    P.wait("sync", dPar, dPar.n)
    P.wait("sync", dSP, dSP.n)

    with es, nc.Block() as block:
        @block.sync
        def _(sync):
            for fn in P.streams["sync"]:
                fn(sync)

        @block.scalar
        def _(scalar):
            for fn in P.streams["scalar"]:
                fn(scalar)

        @block.vector
        def _(vector):
            for fn in P.streams["vector"]:
                fn(vector)

        @block.tensor
        def _(tensor):
            for fn in P.streams["tensor"]:
                fn(tensor)

        @block.gpsimd
        def _(gpsimd):
            for fn in P.streams["gpsimd"]:
                fn(gpsimd)

    return nc


def kernel(L0_attn_x, L0_attn_q_w, L0_attn_k_w, L0_attn_v_w, L0_attn_o_w,
           L0_ffn_W1, L0_ffn_W2):
    x = np.ascontiguousarray(np.asarray(L0_attn_x, dtype=np.float32))
    qwa = np.ascontiguousarray(np.asarray(L0_attn_q_w, dtype=np.float32))
    kwa = np.ascontiguousarray(np.asarray(L0_attn_k_w, dtype=np.float32))
    vwa = np.ascontiguousarray(np.asarray(L0_attn_v_w, dtype=np.float32))
    owa = np.ascontiguousarray(np.asarray(L0_attn_o_w, dtype=np.float32))
    w1a = np.ascontiguousarray(np.asarray(L0_ffn_W1, dtype=np.float32))
    w2a = np.ascontiguousarray(np.asarray(L0_ffn_W2, dtype=np.float32))

    consts = np.zeros((128, 8), dtype=np.float32)
    consts[:, C_ONE] = 1.0
    consts[:, C_SCALE] = SCALE
    consts[:, C_SHIFT] = SHIFT
    consts[:, C_IMEAN] = 1.0 / D
    consts[:, C_127] = 127.0
    consts[:, C_HALF] = 0.5
    consts[:, C_NHALF] = -0.5
    consts[:, C_EPS] = EPS
    ident = np.eye(128, dtype=ml_dtypes.bfloat16)
    identf = np.eye(128, dtype=np.float32)
    onescol = np.ones((128, 1), dtype=np.float32)
    onesrow = np.ones((1, 128), dtype=np.float32)

    nc = build_core()

    in_maps = []
    for c in range(8):
        b, hf = c // 2, c % 2
        xbc = np.concatenate([x[b, hf * TOK:(hf + 1) * TOK, :],
                              x[b, (1 - hf) * TOK:(2 - hf) * TOK, :]], axis=0)
        in_maps.append({
            "xb": np.ascontiguousarray(xbc), "qw": qwa, "kw": kwa, "vw": vwa,
            "ow": owa, "w1": w1a, "w2": w2a, "consts_hbm": consts, "ident_hbm": ident,
            "onescol_hbm": onescol, "onesrow_hbm": onesrow, "identf_hbm": identf,
        })

    res = run_bass_kernel_spmd(nc, in_maps, core_ids=list(range(8)))

    out = np.empty((B, S, D), dtype=np.float32)
    for c in range(8):
        b, hf = c // 2, c % 2
        out[b, hf * TOK:(hf + 1) * TOK, :] = res.results[c]["out"]
    return out
